# version 01 (final)
# baseline (speedup 1.0000x reference)
import numpy as np
import ml_dtypes
from contextlib import ExitStack
import concourse.bass as bass
import concourse.mybir as mybir
from concourse.bass_utils import run_bass_kernel_spmd

F32 = mybir.dt.float32
BF16 = mybir.dt.bfloat16
I32 = mybir.dt.int32
AF = mybir.ActivationFunctionType
ALU = mybir.AluOpType
AX = mybir.AxisListType
BF16NP = ml_dtypes.bfloat16

D = 2048
TOK = 2048
NT = TOK // 128
NCH = D // 128
OI = 5504
OT = OI // 128
DSH = 256
OSH = OI // 8
EPS = 1e-5
SCALE0 = float(128.0 ** -0.5)

DBLK = [128, 128, 128, 128, 128, 48]


def _perm_half(h):
    p = []
    for s in range(4 * h, 4 * h + 4):
        p += list(range(s * 512, s * 512 + 512, 2))
        p += list(range(s * 512 + 1, s * 512 + 512, 2))
    return np.array(p, dtype=np.int64)



DBG_L0 = False
L0_PHASES = ("ln", "att", "mat", "ffn")

def build_l0(detect_races=False):
    PH = L0_PHASES
    nc = bass.Bass("TRN2", target_bir_lowering=False, debug=False,
                   detect_race_conditions=detect_races)

    x_p = nc.dram_tensor("x_p", [TOK, D], F32, kind="ExternalInput").ap()
    gidx = nc.dram_tensor("gidx", [OI, DSH], I32, kind="ExternalInput").ap()
    uidx = nc.dram_tensor("uidx", [OI, DSH], I32, kind="ExternalInput").ap()
    didx = nc.dram_tensor("didx", [D, OSH], I32, kind="ExternalInput").ap()
    lut_b = nc.dram_tensor("lut_b", [128, 16], F32, kind="ExternalInput").ap()
    ident_bf = nc.dram_tensor("ident_bf", [128, 128], BF16, kind="ExternalInput").ap()
    cns = nc.dram_tensor("cns", [128, 4], F32, kind="ExternalInput").ap()

    x1_o = nc.dram_tensor("x1_o", [TOK, D], F32, kind="ExternalOutput").ap()

    x1a_d = nc.dram_tensor("x1a_d", [TOK, D], F32,
                           kind="ExternalOutput" if DBG_L0 else "Internal").ap()
    if DBG_L0:
        xnT_dump = nc.dram_tensor("xnT_dump", [128, NCH * TOK], BF16, kind="ExternalOutput").ap()
        hT_dump = nc.dram_tensor("hT_dump", [128, OT * 1024], BF16, kind="ExternalOutput").ap()
    agg_in = nc.dram_tensor("agg_in", [2 * OT * 128, 128], BF16, kind="Internal").ap()
    agu_in = nc.dram_tensor("agu_in", [2 * OT * 128, 128], BF16, kind="Internal").ap()
    agd_in = nc.dram_tensor("agd_in", [OSH, D], BF16, kind="Internal").ap()
    agg_out = nc.dram_tensor("agg_out", [8 * 2 * OT * 128, 128], BF16, kind="Internal").ap()
    agu_out = nc.dram_tensor("agu_out", [8 * 2 * OT * 128, 128], BF16, kind="Internal").ap()
    agd_out = nc.dram_tensor("agd_out", [OI, D], BF16, kind="Internal").ap()

    mat_tiles = [("g", j) for j in range(OT)] + [("u", j) for j in range(OT)] + [("d", t) for t in range(NCH)]

    def blocks_of(kind):
        return 2 if kind in ("g", "u") else 6

    cumblk = [0]
    for k, _ in mat_tiles:
        cumblk.append(cumblk[-1] + blocks_of(k))
    NBLK = cumblk[-1]

    with ExitStack() as ctx:
        sb = lambda n, s, d: ctx.enter_context(nc.sbuf_tensor(n, s, d))
        psa = lambda n, s, d: ctx.enter_context(nc.psum_tensor(n, s, d))
        sem = lambda n: ctx.enter_context(nc.semaphore(n))

        xnT = sb("xnT", [128, NCH * TOK], BF16)
        hT = sb("hT", [128, OT * 1024], BF16)
        idxb = [sb(f"idxb{i}", [128, 688], I32) for i in range(2)]
        accb = sb("accb", [128, 688], BF16)
        eqb = sb("eqb", [128, 688], BF16)
        stg = [sb(f"stg{i}", [128, 128], BF16) for i in range(4)]
        xbig = [sb(f"xbig{i}", [128, D], F32) for i in range(2)]
        xnb = sb("xnb", [128, D], BF16)
        nrm = [sb(f"nrm{i}", [128, 12], F32) for i in range(2)]
        Ebf = [sb(f"Ebf{i}", [128, 256], BF16) for i in range(2)]
        at_sb = sb("at_sb", [128, 512], BF16)
        gtm_sb = sb("gtm_sb", [128, 256], BF16)
        rinvb = sb("rinvb", [128, 8], F32)
        wgb = [sb(f"wgb{i}", [128, NCH * 128], BF16) for i in range(2)]
        wub = [sb(f"wub{i}", [128, NCH * 128], BF16) for i in range(2)]
        wdb = [sb(f"wdb{i}", [128, 512], BF16) for i in range(2)]
        silb = [sb(f"silb{i}", [128, 512], F32) for i in range(2)]
        lutt = sb("lutt", [128, 16], F32)
        cnst = sb("cnst", [128, 4], F32)
        identb = sb("identb", [128, 128], BF16)

        pb = [psa(f"pb{i}", [128, 512], F32) for i in range(8)]
        pbbf = [p[:, :].bitcast(BF16) for p in pb]
        Ef = [pb[5][:, 0:256], pb[5][:, 256:512]]

        m_ld = [sem(f"m_ld{i}") for i in range(2)]
        m_v = sem("m_v"); m_tp = sem("m_tp"); m_sc = sem("m_sc")
        m_st = [sem(f"m_st{i}") for i in range(4)]
        s_ag = sem("s_ag")
        l_ld = [sem(f"l_ld{i}") for i in range(2)]
        l_ssq = sem("l_ssq"); l_sqf = sem("l_sqf"); l_var = sem("l_var"); l_sq = sem("l_sq")
        l_xn = sem("l_xn"); l_tpb = sem("l_tpb"); l_tev = sem("l_tev")
        a_ld = [sem(f"a_ld{i}") for i in range(2)]
        a_pe1 = sem("a_pe1"); a_mx = sem("a_mx"); a_exp = sem("a_exp")
        a_sm = sem("a_sm"); a_tp = sem("a_tp"); a_tev = sem("a_tev"); a_pe2 = sem("a_pe2")
        a_ev = sem("a_ev")
        a_st = [sem(f"a_st{i}") for i in range(2)]
        f_wg = [sem(f"f_wg{i}") for i in range(2)]
        f_wu = [sem(f"f_wu{i}") for i in range(2)]
        f_pe = sem("f_pe"); f_sc = sem("f_sc"); f_v = sem("f_v")
        f_wd = [sem(f"f_wd{i}") for i in range(2)]
        f_dj = sem("f_dj"); f_dpe = sem("f_dpe"); f_dev = sem("f_dev")
        f_x1l = [sem(f"f_x1l{i}") for i in range(4)]
        f_st = [sem(f"f_st{i}") for i in range(4)]
        c_ld = sem("c_ld")

        def slotval(i, nslots):
            return 16 * (i // nslots + 1)
        def mst_count(slot, nblocks):
            return 16 * len([b for b in range(nblocks) if b % 4 == slot])

        block = ctx.enter_context(nc.Block())

        def head_list(p):
            return [p + 2 * i for i in range(8)]

        def q_fpe(j, g, m):
            return j * 4 + g * 2 + m

        @block.sync
        def _(sync):
            sync.dma_start(out=lutt[:, :], in_=lut_b[:, :]).then_inc(c_ld, 16)
            sync.dma_start(out=identb[:, :], in_=ident_bf[:, :]).then_inc(c_ld, 16)
            sync.dma_start(out=cnst[:, :], in_=cns[:, :]).then_inc(c_ld, 16)
            for i in range(NT):
                if i >= 2:
                    sync.wait_ge(l_xn, i - 1)
                    sync.wait_ge(l_ssq, 4 * (i - 1))
                sync.dma_start(out=xbig[i % 2][:, :], in_=x_p[i * 128:(i + 1) * 128, :]).then_inc(l_ld[i % 2], 16)
            for sp in range(8) if "att" in PH else []:
                t0 = sp * 256
                if sp == 0:
                    sync.wait_ge(l_xn, NT)
                    sync.wait_ge(l_ssq, 4 * NT)
                else:
                    sync.wait_ge(a_st[0], 16 * sp)
                    sync.wait_ge(a_st[1], 16 * sp)
                for qh in range(2):
                    sync.dma_start(out=xbig[qh][:, :],
                                   in_=x_p[t0 + qh * 128: t0 + (qh + 1) * 128, :]).then_inc(a_ld[qh], 16)
            for i, (k, t) in enumerate(mat_tiles if "mat" in PH else []):
                if i >= 2:
                    sync.wait_ge(m_v, i - 1)
                dst = idxb[i % 2]
                sem_i = m_ld[i % 2]
                if k == "g":
                    sync.dma_start(out=dst[:, 0:DSH], in_=gidx[t * 128:(t + 1) * 128, :]).then_inc(sem_i, 16)
                elif k == "u":
                    sync.dma_start(out=dst[:, 0:DSH], in_=uidx[t * 128:(t + 1) * 128, :]).then_inc(sem_i, 16)
                else:
                    sync.dma_start(out=dst[:, 0:OSH], in_=didx[t * 128:(t + 1) * 128, :]).then_inc(sem_i, 16)
            agg5 = agg_out.rearrange("(r c j p) f -> r c j p f", r=8, c=2, j=OT, p=128)
            agu5 = agu_out.rearrange("(r c j p) f -> r c j p f", r=8, c=2, j=OT, p=128)
            nw = 0
            for th in range(2) if "ffn" in PH else []:
                for j in range(OT):
                    if th == 0 and j == 0:
                        sync.wait_ge(s_ag, 2)
                    if nw >= 2:
                        if j >= 2:
                            pth, pj = th, j - 2
                        else:
                            pth, pj = th - 1, OT - 2 + j
                        sync.wait_ge(f_pe, pth * OT * 4 + pj * 4 + 4)
                    sync.dma_start(out=wgb[j % 2][:, :].rearrange("p (r c f) -> p r c f", r=8, c=2),
                                   in_=agg5[:, :, j, :, :].rearrange("r c p f -> p r c f")).then_inc(f_wg[j % 2], 16)
                    sync.dma_start(out=wub[j % 2][:, :].rearrange("p (r c f) -> p r c f", r=8, c=2),
                                   in_=agu5[:, :, j, :, :].rearrange("r c p f -> p r c f")).then_inc(f_wu[j % 2], 16)
                    nw += 1
                if th == 0:
                    sync.wait_ge(s_ag, 3)
                for dp in range(4):
                    for j in range(OT):
                        if dp * OT + j >= 2:
                            if j >= 2:
                                sync.wait_ge(f_dj, th * 4 * (OT - 1) + dp * (OT - 1) + j - 1)
                            else:
                                sync.wait_ge(f_dpe, th * 32 + dp * 8)
                        sync.dma_start(out=wdb[j % 2][:, :],
                                       in_=agd_out[j * 128:(j + 1) * 128, dp * 512:(dp + 1) * 512]).then_inc(f_wd[j % 2], 16)
                    for s in range(8):
                        gi = th * 32 + dp * 8 + s
                        if gi == 0:
                            sync.wait_ge(a_st[0], 16 * 8)
                            sync.wait_ge(a_st[1], 16 * 8)
                        if gi >= 4:
                            sync.wait_ge(f_dev, gi - 3)
                        r0 = th * 1024 + s * 128
                        sync.dma_start(out=xbig[0][:, (gi % 4) * 512:(gi % 4 + 1) * 512],
                                       in_=x1a_d[r0:r0 + 128, dp * 512:(dp + 1) * 512]).then_inc(f_x1l[gi % 4], 16)
            if DBG_L0:
                sync.wait_ge(l_tev, NT * NCH)
                sync.dma_start(out=xnT_dump[:, :], in_=xnT[:, :]).then_inc(c_ld, 16)
                if "ffn" in PH:
                    sync.wait_ge(f_v, 2 * OT * 2)
                    sync.dma_start(out=hT_dump[:, :], in_=hT[:, :]).then_inc(c_ld, 16)
                    sync.wait_ge(c_ld, 16 * 5)
                else:
                    sync.wait_ge(c_ld, 16 * 4)
            if "ffn" in PH:
                for q in range(4):
                    sync.wait_ge(f_st[q], 16 * 16)
            elif DBG_L0:
                sync.wait_ge(c_ld, 16 * 4)

        @block.gpsimd
        def _(gpsimd):
            for sp in range(8) if "att" in PH else []:
                t0 = sp * 256
                for qh in range(2):
                    gpsimd.wait_ge(a_ev, sp * 16 + 16)
                    gpsimd.dma_start(out=x1a_d[t0 + qh * 128: t0 + (qh + 1) * 128, :],
                                     in_=xbig[qh][:, :]).then_inc(a_st[qh], 16)
            blk = 0
            for i, (k, t) in enumerate(mat_tiles if "mat" in PH else []):
                nb = blocks_of(k)
                for b in range(nb):
                    gpsimd.wait_ge(m_sc, blk + 1)
                    s = stg[blk % 4]
                    if k == "g":
                        dst = agg_in[(b * OT + t) * 128:(b * OT + t) * 128 + 128, :]
                        gpsimd.dma_start(out=dst, in_=s[:, :]).then_inc(m_st[blk % 4], 16)
                    elif k == "u":
                        dst = agu_in[(b * OT + t) * 128:(b * OT + t) * 128 + 128, :]
                        gpsimd.dma_start(out=dst, in_=s[:, :]).then_inc(m_st[blk % 4], 16)
                    else:
                        w = DBLK[b]
                        dst = agd_in[b * 128:b * 128 + w, t * 128:(t + 1) * 128]
                        gpsimd.dma_start(out=dst, in_=s[0:w, :]).then_inc(m_st[blk % 4], 16)
                    blk += 1
            if "mat" not in PH:
                return
            for q in range(4):
                gpsimd.wait_ge(m_st[q], mst_count(q, 2 * OT))
            gpsimd.collective_compute("AllGather", ALU.bypass, replica_groups=[list(range(8))],
                                      ins=[agg_in[:, :]], outs=[agg_out[:, :]]).then_inc(s_ag)
            for q in range(4):
                gpsimd.wait_ge(m_st[q], mst_count(q, 4 * OT))
            gpsimd.collective_compute("AllGather", ALU.bypass, replica_groups=[list(range(8))],
                                      ins=[agu_in[:, :]], outs=[agu_out[:, :]]).then_inc(s_ag)
            for q in range(4):
                gpsimd.wait_ge(m_st[q], mst_count(q, NBLK))
            gpsimd.collective_compute("AllGather", ALU.bypass, replica_groups=[list(range(8))],
                                      ins=[agd_in[:, :]], outs=[agd_out[:, :]]).then_inc(s_ag)
            for th in range(2) if "ffn" in PH else []:
                for dp in range(4):
                    for s in range(8):
                        gi = th * 32 + dp * 8 + s
                        gpsimd.wait_ge(f_dev, gi + 1)
                        r0 = th * 1024 + s * 128
                        gpsimd.dma_start(out=x1_o[r0:r0 + 128, dp * 512:(dp + 1) * 512],
                                         in_=xbig[1][:, (gi % 4) * 512:(gi % 4 + 1) * 512]).then_inc(f_st[gi % 4], 16)
            if "ffn" in PH:
                for q in range(4):
                    gpsimd.wait_ge(f_st[q], 16 * 16)

        @block.scalar
        def _(scalar):
            scalar.memzero(silb[1][:, :])
            for f in (AF.Square, AF.Sqrt, AF.Sqrt):
                scalar.activation(out=silb[1][:, :], in_=silb[1][:, :], func=f)
            for i in range(NT):
                scalar.wait_ge(l_ld[i % 2], slotval(i, 2))
                if i == 0:
                    scalar.wait_ge(c_ld, 48)
                for q in range(4):
                    gq = 4 * i + q
                    if gq >= 2:
                        scalar.wait_ge(l_sqf, gq - 1)
                    ins = scalar.activation(out=pb[2 + gq % 2][:, :],
                                            in_=xbig[i % 2][:, q * 512:(q + 1) * 512],
                                            func=AF.Square).then_inc(l_ssq, 1)
                scalar.wait_ge(l_var, i + 1)
                scalar.activation(out=nrm[i % 2][:, 10:11], in_=nrm[i % 2][:, 9:10],
                                  func=AF.Sqrt, bias=cnst[:, 0:1]).then_inc(l_sq, 1)
                for b in range(NCH):
                    gb = i * NCH + b
                    scalar.wait_ge(l_tpb, gb + 1)
                    scalar.activation(out=xnT[:, b * TOK + i * 128: b * TOK + (i + 1) * 128],
                                      in_=pbbf[b % 2][:, 0:128], func=AF.Copy).then_inc(l_tev, 1)
            if "att" in PH:
                scalar.activation(out=silb[1][:, :], in_=silb[1][:, :], func=AF.Exp)
                scalar.activation(out=silb[1][:, :], in_=silb[1][:, :], func=AF.Exp)
            hk = 0
            for sp in range(8) if "att" in PH else []:
                for h in head_list(sp % 2):
                    scalar.wait_ge(a_mx, hk + 1)
                    if hk >= 1:
                        scalar.wait_ge(a_sm, hk)
                    for qh in range(2):
                        ins = scalar.activation(out=Ef[qh], in_=pb[qh][:, 0:256], func=AF.Exp,
                                                bias=rinvb[:, 4 * qh + 1: 4 * qh + 2], scale=SCALE0)
                        if qh == 1:
                            ins.then_inc(a_exp, 1)
                    scalar.wait_ge(a_tp, hk + 1)
                    if hk >= 1:
                        scalar.wait_ge(a_pe2, hk)
                    scalar.activation(out=gtm_sb[:, :], in_=pbbf[2][:, 0:256], func=AF.Copy)
                    scalar.activation(out=at_sb[:, 0:256], in_=pbbf[3][:, 0:256], func=AF.Copy)
                    scalar.activation(out=at_sb[:, 256:512], in_=pbbf[3][:, 256:512],
                                      func=AF.Copy).then_inc(a_tev, 1)
                    hk += 1
            blk = 0
            for i, (k, t) in enumerate(mat_tiles if "mat" in PH else []):
                nb = blocks_of(k)
                for b in range(nb):
                    scalar.wait_ge(m_tp, blk + 1)
                    if blk >= 4:
                        scalar.wait_ge(m_st[blk % 4], 16 * ((blk - 4) // 4 + 1))
                    w = 128 if k in ("g", "u") else DBLK[b]
                    bank = (blk % 2) if k in ("g", "u") else (4 + blk % 2)
                    scalar.activation(out=stg[blk % 4][0:w, :], in_=pbbf[bank][0:w, 0:128],
                                      func=AF.Copy).then_inc(m_sc, 1)
                    blk += 1
            if "ffn" in PH:
                scalar.activation(out=silb[1][:, :], in_=silb[1][:, :], func=AF.Sigmoid)
                scalar.activation(out=silb[1][:, :], in_=silb[1][:, :], func=AF.Sigmoid)
            for th in range(2) if "ffn" in PH else []:
                base = th * OT * 4
                for j in range(OT):
                    for g in range(2):
                        seq = th * OT * 2 + j * 2 + g
                        scalar.wait_ge(f_pe, base + q_fpe(j, g, 0) + 1)
                        if seq >= 2:
                            scalar.wait_ge(f_v, seq - 1)
                        gbank = (j % 2) * 2
                        scalar.activation(out=silb[seq % 2][:, :], in_=pb[gbank][:, :],
                                          func=AF.Sigmoid).then_inc(f_sc, 1)

        @block.vector
        def _(vector):
            def emit_xn(j):
                nj = nrm[j % 2]
                if j >= 1:
                    vector.wait_ge(l_tpb, NCH * j)
                vector.tensor_scalar(out=xnb[:, :], in0=xbig[j % 2][:, :], scalar1=nj[:, 1:2],
                                     scalar2=nj[:, 11:12], op0=ALU.add,
                                     op1=ALU.mult).then_inc(l_xn, 1)

            for i in range(NT):
                vector.wait_ge(l_ld[i % 2], slotval(i, 2))
                if i >= 2:
                    vector.wait_ge(l_xn, i - 1)
                n = nrm[i % 2]
                vector.reduce_sum(n[:, 0:1], xbig[i % 2][:, :], axis=AX.X)
                if i >= 1:
                    emit_xn(i - 1)
                else:
                    vector.tensor_copy(out=silb[0][:, :], in_=xbig[0][:, 0:512])
                vector.wait_ge(l_ssq, 4 * i + 1)
                vector.reduce_sum(n[:, 2:3], pb[2][:, :], axis=AX.X).then_inc(l_sqf, 1)
                vector.tensor_scalar(out=n[:, 1:2], in0=n[:, 0:1], scalar1=-1.0 / D,
                                     scalar2=None, op0=ALU.mult)
                vector.wait_ge(l_ssq, 4 * i + 2)
                vector.reduce_sum(n[:, 3:4], pb[3][:, :], axis=AX.X).then_inc(l_sqf, 1)
                vector.tensor_tensor(out=n[:, 8:9], in0=n[:, 1:2], in1=n[:, 1:2], op=ALU.mult)
                vector.wait_ge(l_ssq, 4 * i + 3)
                vector.reduce_sum(n[:, 4:5], pb[2][:, :], axis=AX.X).then_inc(l_sqf, 1)
                vector.wait_ge(l_ssq, 4 * i + 4)
                vector.reduce_sum(n[:, 5:6], pb[3][:, :], axis=AX.X).then_inc(l_sqf, 1)
                vector.tensor_copy(out=silb[0][:, :], in_=xbig[i % 2][:, 512:1024])
                vector.reduce_sum(n[:, 6:7], n[:, 2:6], axis=AX.X)
                vector.tensor_copy(out=silb[0][:, :], in_=xbig[i % 2][:, 0:512])
                if i >= 2:
                    vector.wait_ge(l_sq, i - 1)
                vector.scalar_tensor_tensor(out=n[:, 9:10], in0=n[:, 6:7], scalar=1.0 / D,
                                            in1=n[:, 8:9], op0=ALU.mult,
                                            op1=ALU.subtract).then_inc(l_var, 1)
                vector.wait_ge(l_sq, i + 1)
                vector.reciprocal(out=n[:, 11:12], in_=n[:, 10:11])
            emit_xn(NT - 1)
            hk = 0
            for sp in range(8) if "att" in PH else []:
                for hi, h in enumerate(head_list(sp % 2)):
                    vector.wait_ge(a_pe1, hk + 1)
                    vector.reduce_max(rinvb[:, 0:1], pb[0][:, 0:256], axis=AX.X)
                    vector.reduce_max(rinvb[:, 4:5], pb[1][:, 0:256], axis=AX.X)
                    vector.tensor_scalar(out=rinvb[:, 1:2], in0=rinvb[:, 0:1], scalar1=-SCALE0,
                                         scalar2=None, op0=ALU.mult)
                    vector.tensor_copy(out=silb[0][:, :], in_=xbig[0][:, 0:512])
                    vector.tensor_scalar(out=rinvb[:, 5:6], in0=rinvb[:, 4:5], scalar1=-SCALE0,
                                         scalar2=None, op0=ALU.mult).then_inc(a_mx, 1)
                    vector.wait_ge(a_exp, hk + 1)
                    if hk >= 1:
                        vector.wait_ge(a_tp, hk)
                    vector.reduce_sum(rinvb[:, 2:3], Ef[0], axis=AX.X)
                    vector.reduce_sum(rinvb[:, 6:7], Ef[1], axis=AX.X)
                    vector.reciprocal(out=rinvb[:, 3:4], in_=rinvb[:, 2:3])
                    vector.tensor_copy(out=Ebf[0][:, :], in_=Ef[0])
                    vector.reciprocal(out=rinvb[:, 7:8], in_=rinvb[:, 6:7])
                    vector.tensor_copy(out=Ebf[1][:, :], in_=Ef[1]).then_inc(a_sm, 1)
                    vector.wait_ge(a_pe2, hk + 1)
                    vector.wait_ge(a_ld[0], 16 * (sp + 1))
                    vector.wait_ge(a_ld[1], 16 * (sp + 1))
                    for qh in range(2):
                        vector.scalar_tensor_tensor(
                            out=xbig[qh][:, h * 128:(h + 1) * 128],
                            in0=pb[4][:, qh * 128:(qh + 1) * 128],
                            scalar=rinvb[:, 4 * qh + 3:4 * qh + 4],
                            in1=xbig[qh][:, h * 128:(h + 1) * 128],
                            op0=ALU.mult, op1=ALU.add).then_inc(a_ev, 1)
                    hk += 1
            for i, (k, t) in enumerate(mat_tiles if "mat" in PH else []):
                W = DSH if k in ("g", "u") else OSH
                vector.wait_ge(m_ld[i % 2], slotval(i, 2))
                if i >= 1:
                    vector.wait_ge(m_tp, cumblk[i])
                vector.tensor_scalar(out=accb[:, 0:W], in0=idxb[i % 2][:, 0:W], scalar1=0.0,
                                     scalar2=lutt[:, 0:1], op0=ALU.is_equal, op1=ALU.mult)
                for kk in range(1, 16):
                    vector.tensor_scalar(out=eqb[:, 0:W], in0=idxb[i % 2][:, 0:W], scalar1=float(kk),
                                         scalar2=lutt[:, kk:kk + 1], op0=ALU.is_equal, op1=ALU.mult)
                    ins = vector.tensor_tensor(out=accb[:, 0:W], in0=accb[:, 0:W], in1=eqb[:, 0:W],
                                               op=ALU.add)
                ins.then_inc(m_v, 1)
            for th in range(2) if "ffn" in PH else []:
                base = th * OT * 4
                for j in range(OT):
                    for g in range(2):
                        seq = th * OT * 2 + j * 2 + g
                        gbank = (j % 2) * 2
                        ubank = (j % 2) * 2 + 1
                        vector.wait_ge(f_sc, seq + 1)
                        if th > 0 and j == 0 and g == 0:
                            vector.wait_ge(f_dpe, th * 32)
                        vector.tensor_tensor(out=silb[seq % 2][:, :], in0=pb[gbank][:, :],
                                             in1=silb[seq % 2][:, :], op=ALU.mult)
                        vector.wait_ge(f_pe, base + q_fpe(j, g, 1) + 1)
                        vector.tensor_tensor(out=hT[:, j * 1024 + g * 512: j * 1024 + (g + 1) * 512],
                                             in0=silb[seq % 2][:, :], in1=pb[ubank][:, :],
                                             op=ALU.mult).then_inc(f_v, 1)
                for dp in range(4):
                    for s in range(8):
                        gi = th * 32 + dp * 8 + s
                        vector.wait_ge(f_dpe, gi + 1)
                        vector.wait_ge(f_x1l[gi % 4], slotval(gi, 4))
                        if gi >= 4:
                            vector.wait_ge(f_st[gi % 4], 16 * (gi // 4))
                        vector.tensor_tensor(out=xbig[1][:, (gi % 4) * 512:(gi % 4 + 1) * 512],
                                             in0=pb[s][:, :],
                                             in1=xbig[0][:, (gi % 4) * 512:(gi % 4 + 1) * 512],
                                             op=ALU.add).then_inc(f_dev, 1)

        @block.tensor
        def _(tensor):
            tensor.wait_ge(c_ld, 48)
            for i in range(NT):
                tensor.wait_ge(l_xn, i + 1)
                for b in range(NCH):
                    gb = i * NCH + b
                    if gb >= 2:
                        tensor.wait_ge(l_tev, gb - 1)
                    tensor.transpose(out=pbbf[b % 2][:, 0:128], in_=xnb[:, b * 128:(b + 1) * 128],
                                     identity=identb[:, :]).then_inc(l_tpb, 1)
            hk = 0
            for sp in range(8) if "att" in PH else []:
                seg = sp // 2
                par = sp % 2
                t0 = sp * 256
                for h in head_list(par):
                    gT = xnT[:, h * TOK + t0: h * TOK + t0 + 256]
                    if hk == 0:
                        tensor.wait_ge(l_tev, NT * NCH)
                        tensor.wait_ge(l_xn, NT)
                    if hk >= 1:
                        tensor.wait_ge(a_exp, hk)
                    for qh in range(2):
                        ins = tensor.matmul(out=pb[qh][:, 0:256],
                                            lhsT=xnT[:, h * TOK + t0 + qh * 128: h * TOK + t0 + (qh + 1) * 128],
                                            rhs=gT, start=True, stop=True)
                    ins.then_inc(a_pe1, 1)
                    if hk >= 1:
                        tensor.wait_ge(a_tev, hk)
                    for kc in range(2):
                        tensor.transpose(out=pbbf[2][:, kc * 128:(kc + 1) * 128],
                                         in_=xnT[:, h * TOK + t0 + kc * 128: h * TOK + t0 + (kc + 1) * 128],
                                         identity=identb[:, :])
                    tensor.wait_ge(a_sm, hk + 1)
                    for kc in range(2):
                        for qh in range(2):
                            ins = tensor.transpose(out=pbbf[3][:, kc * 256 + qh * 128: kc * 256 + (qh + 1) * 128],
                                                   in_=Ebf[qh][:, kc * 128:(kc + 1) * 128],
                                                   identity=identb[:, :])
                    ins.then_inc(a_tp, 1)
                    tensor.wait_ge(a_tev, hk + 1)
                    if hk >= 1:
                        tensor.wait_ge(a_ev, 2 * hk)
                    for qh in range(2):
                        for kc in range(2):
                            ins = tensor.matmul(out=pb[4][:, qh * 128:(qh + 1) * 128],
                                                lhsT=at_sb[:, kc * 256 + qh * 128: kc * 256 + (qh + 1) * 128],
                                                rhs=gtm_sb[:, kc * 128:(kc + 1) * 128],
                                                start=(kc == 0), stop=(kc == 1))
                    ins.then_inc(a_pe2, 1)
                    hk += 1
            blk = 0
            first_gu = True
            first_d = True
            for i, (k, t) in enumerate(mat_tiles if "mat" in PH else []):
                tensor.wait_ge(m_v, i + 1)
                nb = blocks_of(k)
                for b in range(nb):
                    if blk >= 2:
                        tensor.wait_ge(m_sc, blk - 1)
                    if k in ("g", "u"):
                        if first_gu:
                            tensor.wait_ge(a_exp, 64)
                            first_gu = False
                        src = accb[:, b * 128:(b + 1) * 128]
                        tensor.transpose(out=pbbf[blk % 2][:, 0:128], in_=src,
                                         identity=identb[:, :]).then_inc(m_tp, 1)
                    else:
                        if first_d:
                            tensor.wait_ge(a_ev, 128)
                            first_d = False
                        w = DBLK[b]
                        src = accb[:, b * 128: b * 128 + w]
                        tensor.transpose(out=pbbf[4 + blk % 2][0:w, 0:128], in_=src,
                                         identity=identb[:, :]).then_inc(m_tp, 1)
                    blk += 1
            for th in range(2) if "ffn" in PH else []:
                base = th * OT * 4
                for j in range(OT):
                    if th == 0 and j == 0:
                        tensor.wait_ge(m_sc, 4 * OT)
                    nslot = [22, 21][j % 2]
                    tensor.wait_ge(f_wg[j % 2], 16 * (th * nslot + j // 2 + 1))
                    tensor.wait_ge(f_wu[j % 2], 16 * (th * nslot + j // 2 + 1))
                    for g in range(2):
                        gbank = (j % 2) * 2
                        ubank = (j % 2) * 2 + 1
                        if th * OT * 2 + j * 2 + g >= 4:
                            vseq = th * OT * 2 + (j - 2) * 2 + g
                            tensor.wait_ge(f_v, vseq + 1)
                        for dc in range(NCH):
                            ins = tensor.matmul(out=pb[gbank][:, :],
                                                lhsT=wgb[j % 2][:, dc * 128:(dc + 1) * 128],
                                                rhs=xnT[:, dc * TOK + th * 1024 + g * 512: dc * TOK + th * 1024 + (g + 1) * 512],
                                                start=(dc == 0), stop=(dc == NCH - 1))
                        ins.then_inc(f_pe, 1)
                        for dc in range(NCH):
                            ins = tensor.matmul(out=pb[ubank][:, :],
                                                lhsT=wub[j % 2][:, dc * 128:(dc + 1) * 128],
                                                rhs=xnT[:, dc * TOK + th * 1024 + g * 512: dc * TOK + th * 1024 + (g + 1) * 512],
                                                start=(dc == 0), stop=(dc == NCH - 1))
                        ins.then_inc(f_pe, 1)
                if th == 0:
                    tensor.wait_ge(f_v, OT * 2)
                else:
                    tensor.wait_ge(f_v, 2 * OT * 2)
                for dp in range(4):
                    for j in range(OT):
                        dj = dp * OT + j
                        nslot = [22, 21][j % 2]
                        tensor.wait_ge(f_wd[j % 2], 16 * (th * 4 * nslot + dp * nslot + j // 2 + 1))
                        for s in range(8):
                            gi = th * 32 + dp * 8 + s
                            if j == 0 and gi >= 8:
                                tensor.wait_ge(f_dev, gi - 7)
                            ins = tensor.matmul(out=pb[s][:, :],
                                                lhsT=hT[:, j * 1024 + s * 128: j * 1024 + (s + 1) * 128],
                                                rhs=wdb[j % 2][:, :],
                                                start=(j == 0), stop=(j == OT - 1))
                            if j == OT - 1:
                                ins.then_inc(f_dpe, 1)
                            elif s == 7:
                                ins.then_inc(f_dj, 1)

    return nc



def _l0_core_inputs(inputs, c):
    b, hf = c // 2, c % 2
    perm = _perm_half(hf)
    x = np.ascontiguousarray(np.asarray(inputs["L0_attn_x"])[b][perm])
    gidx = np.ascontiguousarray(np.asarray(inputs["L0_ffn_gate_idx"])[:, DSH * c:DSH * (c + 1)])
    uidx = np.ascontiguousarray(np.asarray(inputs["L0_ffn_up_idx"])[:, DSH * c:DSH * (c + 1)])
    didx = np.ascontiguousarray(np.asarray(inputs["L0_ffn_down_idx"])[:, OSH * c:OSH * (c + 1)])
    lut_b = np.ascontiguousarray(np.broadcast_to(np.asarray(inputs["L0_ffn_lut"], dtype=np.float32), (128, 16)))
    ident_bf = np.eye(128, dtype=BF16NP)
    cns = np.zeros((128, 4), dtype=np.float32)
    cns[:, 0] = EPS
    return {"x_p": x, "gidx": gidx, "uidx": uidx, "didx": didx, "lut_b": lut_b,
            "ident_bf": ident_bf, "cns": cns}


def kernel(**inputs):
    cores = list(range(8))
    nc0 = build_l0()
    res0 = run_bass_kernel_spmd(nc0, [_l0_core_inputs(inputs, c) for c in cores],
                                core_ids=cores)
    x1 = [res0.results[c]["x1_o"] for c in cores]

    nc1 = build_l1a()
    res1 = run_bass_kernel_spmd(nc1, [_l1a_core_inputs(inputs, c, x1[c]) for c in cores],
                                core_ids=cores)

    nc2 = build_l1b()
    in2 = []
    for c in cores:
        r = res1.results[c]
        in2.append(_l1b_core_inputs(inputs, c, r["xn1_d"], r["Q_d"],
                                    r["acol_d"], r["sum_d"]))
    res2 = run_bass_kernel_spmd(nc2, in2, core_ids=cores)

    out = np.zeros((4, 4096, D), dtype=np.float32)
    for c in cores:
        b, hf = c // 2, c % 2
        out[b][_perm_half(hf)] = res2.results[c]["out_p"]
    return out



def build_l1a(detect_races=False):
    nc = bass.Bass("TRN2", target_bir_lowering=False, debug=False,
                   detect_race_conditions=detect_races)

    x1_i = nc.dram_tensor("x1_i", [TOK, D], F32, kind="ExternalInput").ap()
    W_in_p = nc.dram_tensor("W_in_p", [D, 3 * D], F32, kind="ExternalInput").ap()
    b_inQ = nc.dram_tensor("b_inQ", [128, 16], F32, kind="ExternalInput").ap()
    b_inKV = nc.dram_tensor("b_inKV", [128, 2 * D], F32, kind="ExternalInput").ap()
    gam_c = nc.dram_tensor("gam_c", [128, 16], F32, kind="ExternalInput").ap()
    bet_c = nc.dram_tensor("bet_c", [128, 16], F32, kind="ExternalInput").ap()
    cosd = nc.dram_tensor("cosd", [TOK, 512], F32, kind="ExternalInput").ap()
    sind = nc.dram_tensor("sind", [TOK, 512], F32, kind="ExternalInput").ap()
    ident_bf = nc.dram_tensor("ident_bf", [128, 128], BF16, kind="ExternalInput").ap()
    identf = nc.dram_tensor("identf", [128, 128], F32, kind="ExternalInput").ap()
    ones_bf = nc.dram_tensor("ones_bf", [128, 1], BF16, kind="ExternalInput").ap()
    cns = nc.dram_tensor("cns", [128, 4], F32, kind="ExternalInput").ap()

    Q_d = nc.dram_tensor("Q_d", [D, TOK], BF16, kind="ExternalOutput").ap()
    xn1_d = nc.dram_tensor("xn1_d", [TOK, D], BF16, kind="ExternalOutput").ap()
    sum_d = nc.dram_tensor("sum_d", [TOK, D], F32, kind="ExternalOutput").ap()
    acol_d = nc.dram_tensor("acol_d", [128, 16], F32, kind="ExternalOutput").ap()

    ar_in = nc.dram_tensor("ar_inA", [128, 16], F32, kind="Internal").ap()
    ar_out = nc.dram_tensor("ar_outA", [128, 16], F32, kind="Internal").ap()

    with ExitStack() as ctx:
        sb = lambda n_, s_, d_: ctx.enter_context(nc.sbuf_tensor(n_, s_, d_))
        psa = lambda n_, s_, d_: ctx.enter_context(nc.psum_tensor(n_, s_, d_))
        sem = lambda n_: ctx.enter_context(nc.semaphore(n_))

        T = [sb(f"T{i}", [128, D], F32) for i in range(8)]
        Xn2T = sb("Xn2T", [128, NCH * TOK], BF16)
        wkb = [sb(f"wkb{i}", [128, NCH * 512], BF16) for i in range(2)]
        tmp = [sb(f"tmp{i}", [128, 512], F32) for i in range(4)]
        cosb = [sb(f"cosb{i}", [128, 512], F32) for i in range(2)]
        sinb = [sb(f"sinb{i}", [128, 512], F32) for i in range(2)]
        xn2b = [sb(f"xn2b{i}", [128, D], BF16) for i in range(2)]
        nrm = [sb(f"nrmA{i}", [128, 24], F32) for i in range(2)]
        prodb = sb("prodb", [128, D], BF16)
        asb = sb("asb", [1, D], F32)
        acolsb = sb("acolsb", [128, 16], F32)
        gamc = sb("gamc", [128, 16], F32)
        betc = sb("betc", [128, 16], F32)
        biq = sb("biq", [128, 16], F32)
        bikv = sb("bikv", [128, 2 * D], BF16)
        cnst = sb("cnstA", [128, 4], F32)
        identb = sb("identbA", [128, 128], BF16)
        identfb = sb("identfA", [128, 128], F32)
        onesb = sb("onesbA", [128, 1], BF16)
        junk = sb("junkA", [128, 512], F32)

        pb = [psa(f"pbA{i}", [128, 512], F32) for i in range(8)]
        pbbf = [p[:, :].bitcast(BF16) for p in pb]
        wqb = [T[2][:, :].bitcast(BF16)[:, 0:NCH * 128], T[3][:, :].bitcast(BF16)[:, 0:NCH * 128]]
        qstg = [T[4][:, :].bitcast(BF16)[:, 0:512], T[4][:, :].bitcast(BF16)[:, 512:1024]]

        c_ld = sem("c_ldA")
        n_ld = [sem(f"n_ld{i}") for i in range(2)]
        n_ssq = sem("n_ssq"); n_sqf = sem("n_sqf"); n_var = sem("n_var"); n_sq = sem("n_sq")
        n_xnf = sem("n_xnf"); n_xn = sem("n_xn"); n_tpb = sem("n_tpb"); n_tev = sem("n_tev")
        n_sumv = sem("n_sumv"); n_x1bv = sem("n_x1bv")
        st_sum = [sem(f"st_sum{i}") for i in range(2)]
        st_x1b = [sem(f"st_x1b{i}") for i in range(2)]
        k_w = [sem(f"k_w{i}") for i in range(2)]
        k_pe = sem("k_pe"); k_ev = sem("k_ev")
        r_cs = [sem(f"r_cs{i}") for i in range(2)]
        r_v = sem("r_v"); r_ssq = sem("r_ssq"); r_sqf = sem("r_sqf"); r_nrm = sem("r_nrm")
        r_prod = sem("r_prod"); a_og = sem("a_og"); a_ones = sem("a_ones"); a_asb = sem("a_asb")
        q_w = [sem(f"q_w{i}") for i in range(2)]
        q_pe = sem("q_pe"); q_ev = sem("q_ev")
        st_q = [sem(f"st_q{i}") for i in range(2)]
        cc = sem("ccA"); cc2 = sem("ccA2"); a_aff = sem("a_aff"); a_tp = sem("a_tpA")
        a_colv = sem("a_colv"); b_cv = sem("b_cv")
        st_a = sem("st_a")

        block = ctx.enter_context(nc.Block())

        def slotval(i, nslots):
            return 16 * (i // nslots + 1)

        @block.sync
        def _(sync):
            for dst, src in ((cnst, cns), (identb, ident_bf), (identfb, identf),
                             (onesb, ones_bf), (gamc, gam_c), (betc, bet_c),
                             (biq, b_inQ)):
                sync.dma_start(out=dst[:, :], in_=src[:, :]).then_inc(c_ld, 16)
            for q4 in range(8):
                if q4 == 4:
                    sync.wait_ge(b_cv, 4)
                sync.dma_start(out=tmp[q4 % 4][:, :],
                               in_=b_inKV[:, q4 * 512:(q4 + 1) * 512]).then_inc(c_ld, 16)
            for i in range(NT):
                if i >= 2:
                    sync.wait_ge(n_xn, i - 1)
                    sync.wait_ge(n_ssq, 8 * (i - 1))
                sync.dma_start(out=T[i % 2][:, :], in_=x1_i[i * 128:(i + 1) * 128, :]).then_inc(n_ld[i % 2], 16)
            for i in range(NT):
                if i >= 2:
                    sync.wait_ge(r_v, i - 1)
                sync.dma_start(out=cosb[i % 2][:, :], in_=cosd[i * 128:(i + 1) * 128, :]).then_inc(r_cs[i % 2], 16)
                sync.dma_start(out=sinb[i % 2][:, :], in_=sind[i * 128:(i + 1) * 128, :]).then_inc(r_cs[i % 2], 16)
            for q in range(2):
                sync.wait_ge(st_sum[q], 16 * 8)
                sync.wait_ge(st_x1b[q], 16 * 8)
                sync.wait_ge(st_q[q], 16 * 32)
            sync.wait_ge(st_a, 16)

        @block.gpsimd
        def _(gpsimd):
            for i in range(NT):
                gpsimd.wait_ge(n_sumv, i + 1)
                gpsimd.dma_start(out=sum_d[i * 128:(i + 1) * 128, :],
                                 in_=T[4 + i % 2][:, :]).then_inc(st_sum[i % 2], 16)
                gpsimd.wait_ge(n_x1bv, i + 1)
                gpsimd.dma_start(out=xn1_d[i * 128:(i + 1) * 128, :],
                                 in_=T[6 + i % 2][:, :].bitcast(BF16)[:, 0:D]).then_inc(st_x1b[i % 2], 16)
            for st in range(4):
                for g in range(8):
                    u = st * 8 + g
                    if u >= 2:
                        gpsimd.wait_ge(k_pe, (u - 1) * 4)
                    gpsimd.dma_start(
                        out=wkb[g % 2][:, :].rearrange("p (c f) -> p c f", c=NCH),
                        in_=W_in_p[:, D + g * 512:D + (g + 1) * 512].rearrange("(c p) f -> p c f", p=128),
                    ).then_inc(k_w[g % 2], 16)
            def q_store(u):
                qf_, g_ = u // 4, u % 4
                gpsimd.wait_ge(q_ev, u + 1)
                gpsimd.dma_start(out=Q_d[qf_ * 128:(qf_ + 1) * 128, g_ * 512:(g_ + 1) * 512],
                                 in_=qstg[u % 2][:, :]).then_inc(st_q[u % 2], 16)

            for qf in range(NCH):
                if qf == 0:
                    gpsimd.wait_ge(r_prod, NT)
                    gpsimd.wait_ge(r_ssq, 8 * NT)
                if qf >= 2:
                    gpsimd.wait_ge(q_pe, (qf - 1) * 4)
                gpsimd.dma_start(
                    out=wqb[qf % 2][:, :].rearrange("p (c f) -> p c f", c=NCH),
                    in_=W_in_p[:, qf * 128:(qf + 1) * 128].rearrange("(c p) f -> p c f", p=128),
                ).then_inc(q_w[qf % 2], 16)
                if qf >= 1:
                    for g_ in range(4):
                        q_store((qf - 1) * 4 + g_)
            for g_ in range(4):
                q_store((NCH - 1) * 4 + g_)
            gpsimd.wait_ge(a_colv, 1)
            gpsimd.dma_start(out=ar_in[:, :], in_=acolsb[:, :]).then_inc(cc, 16)
            gpsimd.wait_ge(cc, 16)
            gpsimd.collective_compute("AllReduce", ALU.add,
                                      replica_groups=[[0, 1], [2, 3], [4, 5], [6, 7]],
                                      ins=[ar_in[:, :]], outs=[ar_out[:, :]]).then_inc(cc2)
            gpsimd.wait_ge(cc2, 1)
            gpsimd.dma_start(out=acolsb[:, :], in_=ar_out[:, :]).then_inc(cc, 16)
            gpsimd.wait_ge(a_aff, 1)
            gpsimd.dma_start(out=acol_d[:, :], in_=acolsb[:, :]).then_inc(st_a, 16)
            gpsimd.wait_ge(st_a, 16)

        @block.scalar
        def _(scalar):
            scalar.memzero(junk[:, :])
            for f in (AF.Square, AF.Sqrt, AF.Sqrt):
                scalar.activation(out=junk[:, :], in_=junk[:, :], func=f)
            for i in range(NT):
                scalar.wait_ge(n_ld[i % 2], slotval(i, 2))
                if i == 0:
                    scalar.wait_ge(c_ld, 16 * 7)
                for q in range(4):
                    gq = 8 * i + q
                    if gq >= 2:
                        scalar.wait_ge(n_sqf, gq - 1)
                    scalar.activation(out=pb[2 + gq % 2][:, :],
                                      in_=T[i % 2][:, q * 512:(q + 1) * 512],
                                      func=AF.Square).then_inc(n_ssq, 1)
                scalar.wait_ge(n_var, 2 * i + 1)
                scalar.activation(out=nrm[i % 2][:, 10:11], in_=nrm[i % 2][:, 9:10],
                                  func=AF.Sqrt, bias=cnst[:, 0:1]).then_inc(n_sq, 1)
                scalar.wait_ge(n_xnf, i + 1)
                for q in range(4):
                    gq = 8 * i + 4 + q
                    if gq >= 2:
                        scalar.wait_ge(n_sqf, gq - 1)
                    scalar.activation(out=pb[2 + gq % 2][:, :],
                                      in_=T[2 + i % 2][:, q * 512:(q + 1) * 512],
                                      func=AF.Square).then_inc(n_ssq, 1)
                scalar.wait_ge(n_var, 2 * i + 2)
                scalar.activation(out=nrm[i % 2][:, 22:23], in_=nrm[i % 2][:, 21:22],
                                  func=AF.Sqrt, bias=cnst[:, 0:1]).then_inc(n_sq, 1)
            for i in range(NT):
                scalar.wait_ge(r_v, i + 1)
                for q in range(8):
                    gq = 8 * i + q
                    if gq >= 2:
                        scalar.wait_ge(r_sqf, gq - 1)
                    if q < 4:
                        src = T[i % 4][:, q * 512:(q + 1) * 512]
                    else:
                        src = T[4 + i % 4][:, (q - 4) * 512:(q - 3) * 512]
                    scalar.activation(out=pb[2 + gq % 2][:, :], in_=src,
                                      func=AF.Square).then_inc(r_ssq, 1)
                scalar.wait_ge(r_nrm, 2 * i + 1)
                scalar.activation(out=nrm[i % 2][:, 10:11], in_=nrm[i % 2][:, 9:10],
                                  func=AF.Sqrt).then_inc(n_sq, 1)
                scalar.wait_ge(r_nrm, 2 * i + 2)
                scalar.activation(out=nrm[i % 2][:, 22:23], in_=nrm[i % 2][:, 21:22],
                                  func=AF.Sqrt).then_inc(n_sq, 1)

        @block.vector
        def _(vector):
            vector.wait_ge(c_ld, 16 * 11)
            for q4 in range(4):
                vector.tensor_copy(out=bikv[:, q4 * 512:(q4 + 1) * 512],
                                   in_=tmp[q4][:, :]).then_inc(b_cv, 1)
            vector.wait_ge(c_ld, 16 * 15)
            for q4 in range(4, 8):
                vector.tensor_copy(out=bikv[:, q4 * 512:(q4 + 1) * 512],
                                   in_=tmp[q4 % 4][:, :]).then_inc(b_cv, 1)
            def emit_evacs(j):
                for b in range(NCH):
                    gb = j * NCH + b
                    vector.wait_ge(n_tpb, gb + 1)
                    vector.tensor_scalar(out=Xn2T[:, b * TOK + j * 128: b * TOK + (j + 1) * 128],
                                         in0=pbbf[gb % 2][:, 0:128], scalar1=gamc[:, b:b + 1],
                                         scalar2=betc[:, b:b + 1], op0=ALU.mult,
                                         op1=ALU.add).then_inc(n_tev, 1)

            for i in range(NT):
                p = i % 2
                n = nrm[p]
                if i >= 1:
                    emit_evacs(i - 1)
                vector.wait_ge(n_ld[p], slotval(i, 2))
                if i >= 2:
                    vector.wait_ge(n_xn, i - 1)
                vector.reduce_sum(n[:, 0:1], T[p][:, :], axis=AX.X)
                vector.wait_ge(n_ssq, 8 * i + 1)
                vector.reduce_sum(n[:, 2:3], pb[2][:, :], axis=AX.X).then_inc(n_sqf, 1)
                vector.tensor_scalar(out=n[:, 1:2], in0=n[:, 0:1], scalar1=-1.0 / D,
                                     scalar2=None, op0=ALU.mult)
                vector.wait_ge(n_ssq, 8 * i + 2)
                vector.reduce_sum(n[:, 3:4], pb[3][:, :], axis=AX.X).then_inc(n_sqf, 1)
                vector.tensor_tensor(out=n[:, 8:9], in0=n[:, 1:2], in1=n[:, 1:2], op=ALU.mult)
                vector.wait_ge(n_ssq, 8 * i + 3)
                vector.reduce_sum(n[:, 4:5], pb[2][:, :], axis=AX.X).then_inc(n_sqf, 1)
                vector.wait_ge(n_ssq, 8 * i + 4)
                vector.reduce_sum(n[:, 5:6], pb[3][:, :], axis=AX.X).then_inc(n_sqf, 1)
                vector.tensor_copy(out=junk[:, :], in_=T[p][:, 0:512])
                vector.reduce_sum(n[:, 6:7], n[:, 2:6], axis=AX.X)
                vector.tensor_copy(out=junk[:, :], in_=T[p][:, 0:512])
                vector.scalar_tensor_tensor(out=n[:, 9:10], in0=n[:, 6:7], scalar=1.0 / D,
                                            in1=n[:, 8:9], op0=ALU.mult,
                                            op1=ALU.subtract).then_inc(n_var, 1)
                vector.wait_ge(n_sq, 2 * i + 1)
                vector.reciprocal(out=n[:, 11:12], in_=n[:, 10:11])
                vector.tensor_copy(out=junk[:, :], in_=T[p][:, 0:512])
                vector.tensor_scalar(out=T[2 + p][:, :], in0=T[p][:, :], scalar1=n[:, 1:2],
                                     scalar2=n[:, 11:12], op0=ALU.add,
                                     op1=ALU.mult).then_inc(n_xnf, 1)
                if i >= 2:
                    vector.wait_ge(st_sum[p], slotval(i - 2, 2))
                vector.tensor_tensor(out=T[4 + p][:, :], in0=T[p][:, :], in1=T[2 + p][:, :],
                                     op=ALU.add).then_inc(n_sumv, 1)
                if i >= 2:
                    vector.wait_ge(st_x1b[p], slotval(i - 2, 2))
                vector.tensor_copy(out=T[6 + p][:, :].bitcast(BF16)[:, 0:D],
                                   in_=T[2 + p][:, :]).then_inc(n_x1bv, 1)
                vector.reduce_sum(n[:, 12:13], T[2 + p][:, :], axis=AX.X)
                vector.wait_ge(n_ssq, 8 * i + 5)
                vector.reduce_sum(n[:, 14:15], pb[2][:, :], axis=AX.X).then_inc(n_sqf, 1)
                vector.tensor_scalar(out=n[:, 13:14], in0=n[:, 12:13], scalar1=-1.0 / D,
                                     scalar2=None, op0=ALU.mult)
                vector.wait_ge(n_ssq, 8 * i + 6)
                vector.reduce_sum(n[:, 15:16], pb[3][:, :], axis=AX.X).then_inc(n_sqf, 1)
                vector.tensor_tensor(out=n[:, 20:21], in0=n[:, 13:14], in1=n[:, 13:14], op=ALU.mult)
                vector.wait_ge(n_ssq, 8 * i + 7)
                vector.reduce_sum(n[:, 16:17], pb[2][:, :], axis=AX.X).then_inc(n_sqf, 1)
                vector.wait_ge(n_ssq, 8 * i + 8)
                vector.reduce_sum(n[:, 17:18], pb[3][:, :], axis=AX.X).then_inc(n_sqf, 1)
                vector.tensor_copy(out=junk[:, :], in_=T[p][:, 0:512])
                vector.reduce_sum(n[:, 18:19], n[:, 14:18], axis=AX.X)
                vector.tensor_copy(out=junk[:, :], in_=T[p][:, 0:512])
                vector.scalar_tensor_tensor(out=n[:, 21:22], in0=n[:, 18:19], scalar=1.0 / D,
                                            in1=n[:, 20:21], op0=ALU.mult,
                                            op1=ALU.subtract).then_inc(n_var, 1)
                vector.wait_ge(n_sq, 2 * i + 2)
                vector.reciprocal(out=n[:, 23:24], in_=n[:, 22:23])
                vector.tensor_copy(out=junk[:, :], in_=T[p][:, 0:512])
                if i >= 2:
                    vector.wait_ge(n_tpb, NCH * (i - 1))
                vector.tensor_scalar(out=xn2b[p][:, :], in0=T[2 + p][:, :], scalar1=n[:, 13:14],
                                     scalar2=n[:, 23:24], op0=ALU.add,
                                     op1=ALU.mult).then_inc(n_xn, 1)
            emit_evacs(NT - 1)
            vector.wait_ge(n_ssq, 8 * NT)
            for q in range(2):
                vector.wait_ge(st_sum[q], 16 * 8)
                vector.wait_ge(st_x1b[q], 16 * 8)
            for st in range(4):
                for g in range(8):
                    u = st * 8 + g
                    if st >= 1:
                        vector.wait_ge(r_ssq, 32 * st)
                    for t4 in range(4):
                        e = u * 4 + t4
                        vector.wait_ge(k_pe, e + 1)
                        dst = T[t4] if g < 4 else T[4 + t4]
                        vector.tensor_tensor(
                            out=dst[:, (g % 4) * 512:(g % 4 + 1) * 512],
                            in0=pb[e % 2][:, :],
                            in1=bikv[:, g * 512:(g + 1) * 512],
                            op=ALU.add).then_inc(k_ev, 1)
                for t4 in range(4):
                    i = st * 4 + t4
                    p = i % 2
                    n = nrm[p]
                    K = T[t4]
                    V = T[4 + t4]
                    vector.wait_ge(r_cs[p], 32 * (i // 2 + 1))
                    vector.tensor_tensor(out=tmp[0][:, :], in0=K[:, 0:512], in1=cosb[p][:, :], op=ALU.mult)
                    vector.tensor_tensor(out=tmp[1][:, :], in0=K[:, 0:512], in1=sinb[p][:, :], op=ALU.mult)
                    vector.tensor_tensor(out=tmp[2][:, :], in0=K[:, 512:1024], in1=cosb[p][:, :], op=ALU.mult)
                    vector.tensor_tensor(out=tmp[3][:, :], in0=K[:, 512:1024], in1=sinb[p][:, :], op=ALU.mult)
                    vector.tensor_tensor(out=K[:, 0:512], in0=tmp[0][:, :], in1=tmp[3][:, :], op=ALU.subtract)
                    vector.tensor_tensor(out=K[:, 512:1024], in0=tmp[2][:, :], in1=tmp[1][:, :], op=ALU.add)
                    vector.tensor_tensor(out=tmp[0][:, :], in0=V[:, 0:512], in1=cosb[p][:, :], op=ALU.mult)
                    vector.tensor_tensor(out=tmp[1][:, :], in0=V[:, 0:512], in1=sinb[p][:, :], op=ALU.mult)
                    vector.tensor_tensor(out=tmp[2][:, :], in0=V[:, 512:1024], in1=cosb[p][:, :], op=ALU.mult)
                    vector.tensor_tensor(out=tmp[3][:, :], in0=V[:, 512:1024], in1=sinb[p][:, :], op=ALU.mult)
                    vector.tensor_tensor(out=V[:, 0:512], in0=tmp[0][:, :], in1=tmp[3][:, :], op=ALU.subtract)
                    vector.tensor_tensor(out=V[:, 512:1024], in0=tmp[2][:, :], in1=tmp[1][:, :],
                                         op=ALU.add).then_inc(r_v, 1)
                    for q in range(8):
                        gq = 8 * i + q
                        col = (2 + q) if q < 4 else (14 + q - 4)
                        vector.wait_ge(r_ssq, gq + 1)
                        vector.reduce_sum(n[:, col:col + 1], pb[2 + gq % 2][:, :],
                                          axis=AX.X).then_inc(r_sqf, 1)
                        if q == 3:
                            vector.tensor_copy(out=junk[:, :], in_=K[:, 0:512])
                            vector.reduce_sum(n[:, 9:10], n[:, 2:6], axis=AX.X).then_inc(r_nrm, 1)
                    vector.tensor_copy(out=junk[:, :], in_=K[:, 0:512])
                    vector.reduce_sum(n[:, 21:22], n[:, 14:18], axis=AX.X).then_inc(r_nrm, 1)
                    vector.wait_ge(n_sq, 32 + 2 * i + 2)
                    vector.tensor_tensor(out=n[:, 11:12], in0=n[:, 10:11], in1=n[:, 22:23], op=ALU.mult)
                    vector.tensor_copy(out=junk[:, :], in_=K[:, 0:512])
                    vector.reciprocal(out=n[:, 12:13], in_=n[:, 11:12])
                    vector.tensor_copy(out=junk[:, :], in_=K[:, 0:512])
                    if i >= 1:
                        vector.wait_ge(a_og, i)
                    for qc in range(4):
                        vector.tensor_tensor(out=tmp[qc][:, :], in0=K[:, qc * 512:(qc + 1) * 512],
                                             in1=V[:, qc * 512:(qc + 1) * 512], op=ALU.mult)
                    for qc in range(4):
                        ins = vector.tensor_scalar(out=prodb[:, qc * 512:(qc + 1) * 512],
                                                   in0=tmp[qc][:, :], scalar1=n[:, 12:13],
                                                   scalar2=None, op0=ALU.mult)
                    ins.then_inc(r_prod, 1)
            vector.wait_ge(a_ones, 1)
            for dg in range(4):
                ins = vector.tensor_copy(out=asb[0:1, dg * 512:(dg + 1) * 512],
                                         in_=pb[4 + dg][0:1, :])
            ins.then_inc(a_asb, 1)
            for qf in range(NCH):
                for g in range(4):
                    u = qf * 4 + g
                    vector.wait_ge(q_pe, u + 1)
                    if u >= 2:
                        vector.wait_ge(st_q[u % 2], slotval(u - 2, 2))
                    vector.tensor_scalar(out=qstg[u % 2][:, :], in0=pb[6 + u % 2][:, :],
                                         scalar1=biq[:, qf:qf + 1], scalar2=None,
                                         op0=ALU.add).then_inc(q_ev, 1)
            vector.wait_ge(a_tp, 16)
            vector.tensor_copy(out=acolsb[:, :], in_=pb[5][:, 0:16]).then_inc(a_colv, 1)
            vector.wait_ge(cc, 32)
            vector.tensor_scalar(out=acolsb[:, :], in0=acolsb[:, :], scalar1=0.5, scalar2=0.5,
                                 op0=ALU.mult, op1=ALU.add).then_inc(a_aff, 1)

        @block.tensor
        def _(tensor):
            tensor.wait_ge(c_ld, 16 * 7)
            for i in range(NT):
                tensor.wait_ge(n_xn, i + 1)
                for b in range(NCH):
                    gb = i * NCH + b
                    if gb >= 2:
                        tensor.wait_ge(n_tev, gb - 1)
                    tensor.transpose(out=pbbf[gb % 2][:, 0:128],
                                     in_=xn2b[i % 2][:, b * 128:(b + 1) * 128],
                                     identity=identb[:, :]).then_inc(n_tpb, 1)
            tensor.wait_ge(n_tev, NT * NCH)
            for st in range(4):
                for g in range(8):
                    u = st * 8 + g
                    tensor.wait_ge(k_w[g % 2], 16 * (u // 2 + 1))
                    for t4 in range(4):
                        e = u * 4 + t4
                        if e >= 2:
                            tensor.wait_ge(k_ev, e - 1)
                        for dc in range(NCH):
                            ins = tensor.matmul(
                                out=pb[e % 2][:, :],
                                lhsT=Xn2T[:, dc * TOK + st * 512 + t4 * 128: dc * TOK + st * 512 + (t4 + 1) * 128],
                                rhs=wkb[g % 2][:, dc * 512:(dc + 1) * 512],
                                start=(dc == 0), stop=(dc == NCH - 1))
                        ins.then_inc(k_pe, 1)
                for t4 in range(4):
                    i = st * 4 + t4
                    tensor.wait_ge(r_prod, i + 1)
                    for dg in range(4):
                        ins = tensor.matmul(out=pb[4 + dg][0:1, :], lhsT=onesb[:, :],
                                            rhs=prodb[:, dg * 512:(dg + 1) * 512],
                                            start=(i == 0), stop=(i == NT - 1))
                    if i == NT - 1:
                        ins.then_inc(a_ones, 1)
                    else:
                        ins.then_inc(a_og, 1)
            tensor.wait_ge(a_asb, 1)
            for c in range(NCH):
                tensor.transpose(out=pb[5][:, c:c + 1], in_=asb[0:1, c * 128:(c + 1) * 128],
                                 identity=identfb[0:1, 0:1]).then_inc(a_tp, 1)
            for qf in range(NCH):
                tensor.wait_ge(q_w[qf % 2], 16 * (qf // 2 + 1))
                for g in range(4):
                    u = qf * 4 + g
                    if u >= 2:
                        tensor.wait_ge(q_ev, u - 1)
                    for dc in range(NCH):
                        ins = tensor.matmul(out=pb[6 + u % 2][:, :],
                                            lhsT=wqb[qf % 2][:, dc * 128:(dc + 1) * 128],
                                            rhs=Xn2T[:, dc * TOK + g * 512: dc * TOK + (g + 1) * 512],
                                            start=(dc == 0), stop=(dc == NCH - 1))
                    ins.then_inc(q_pe, 1)

    return nc



def _sigma_perm():
    p = np.zeros(D, dtype=np.int64)
    for i in range(512):
        p[i] = 2 * i
        p[512 + i] = 2 * i + 1
    p[1024:] = np.arange(1024, 2048)
    return p


def _l1a_core_inputs(inputs, c, x1):
    hf = c % 2
    sig = _sigma_perm()
    W = np.asarray(inputs["L1_attn_W_in"], dtype=np.float32)
    Wq = W[:, 0:D][:, sig]
    Wk = W[:, D:2 * D][:, sig]
    Wv = W[:, 2 * D:3 * D][:, sig]
    W_in_p = np.ascontiguousarray(np.concatenate([Wq, Wk, Wv], axis=1))
    b = np.asarray(inputs["L1_attn_b_in"], dtype=np.float32)
    bq = b[0:D][sig]
    bkv = np.concatenate([b[D:2 * D][sig], b[2 * D:3 * D][sig]])
    b_inQ = np.ascontiguousarray(bq.reshape(16, 128).T)
    b_inKV = np.ascontiguousarray(np.broadcast_to(bkv, (128, 2 * D)))
    gam_c = np.ascontiguousarray(np.asarray(inputs["L1_attn_gamma"], dtype=np.float32).reshape(16, 128).T)
    bet_c = np.ascontiguousarray(np.asarray(inputs["L1_attn_beta"], dtype=np.float32).reshape(16, 128).T)
    pos = _perm_half(hf).astype(np.float32)
    invf = (1.0 / (10000.0 ** (np.arange(0, 1024, 2, dtype=np.float32) / np.float32(1024)))).astype(np.float32)
    fr = pos[:, None] * invf[None, :]
    cosd = np.cos(fr).astype(np.float32)
    sind = np.sin(fr).astype(np.float32)
    cns = np.zeros((128, 4), dtype=np.float32)
    cns[:, 0] = EPS
    cns[:, 1] = 1.0
    return {"x1_i": np.ascontiguousarray(x1), "W_in_p": W_in_p, "b_inQ": b_inQ,
            "b_inKV": b_inKV, "gam_c": gam_c, "bet_c": bet_c, "cosd": cosd,
            "sind": sind, "ident_bf": np.eye(128, dtype=BF16NP),
            "identf": np.eye(128, dtype=np.float32),
            "ones_bf": np.ones((128, 1), dtype=BF16NP), "cns": cns}


NOC = 11
NEXP = 9
HT = 1024
NT_H = 8
NDG = 8
BIG_NEG = -1.0e30


def build_l1b(detect_races=False):
    nc = bass.Bass("TRN2", target_bir_lowering=False, debug=False,
                   detect_race_conditions=detect_races)

    xn1_i = nc.dram_tensor("xn1_i", [TOK, D], BF16, kind="ExternalInput").ap()
    Q_i = nc.dram_tensor("Q_i", [D, TOK], BF16, kind="ExternalInput").ap()
    acol_i = nc.dram_tensor("acol_i", [128, 16], F32, kind="ExternalInput").ap()
    sum_i = nc.dram_tensor("sum_i", [TOK, D], F32, kind="ExternalInput").ap()
    gate_wb = nc.dram_tensor("gate_wb", [16, 128, 8], F32, kind="ExternalInput").ap()
    gate_bb = nc.dram_tensor("gate_bb", [128, 8], F32, kind="ExternalInput").ap()
    b_out_b = nc.dram_tensor("b_out_b", [128, D], F32, kind="ExternalInput").ap()
    w1b = nc.dram_tensor("w1b", [NEXP, NOC, NCH, 128, 128], BF16, kind="ExternalInput").ap()
    w3b = nc.dram_tensor("w3b", [NEXP, NOC, NCH, 128, 128], BF16, kind="ExternalInput").ap()
    w2t = nc.dram_tensor("w2t", [NEXP, 1408, D], BF16, kind="ExternalInput").ap()
    W_out_p = nc.dram_tensor("W_out_p", [D, D], F32, kind="ExternalInput").ap()
    ident_bf = nc.dram_tensor("ident_bfB", [128, 128], BF16, kind="ExternalInput").ap()
    cns = nc.dram_tensor("cnsB", [128, 4], F32, kind="ExternalInput").ap()

    out_p = nc.dram_tensor("out_p", [TOK, D], F32, kind="ExternalOutput").ap()

    with ExitStack() as ctx:
        sb = lambda n_, s_, d_: ctx.enter_context(nc.sbuf_tensor(n_, s_, d_))
        psa = lambda n_, s_, d_: ctx.enter_context(nc.psum_tensor(n_, s_, d_))
        sem = lambda n_: ctx.enter_context(nc.semaphore(n_))

        xn1T = sb("xn1Th", [128, NCH * HT], BF16)
        acc = sb("acc", [128, NT_H * D], F32)
        hbuf = sb("hbuf", [128, NOC * HT], BF16)
        Ybuf = hbuf[:, 0:NCH * 512]
        w1c = [sb(f"w1c{i}", [128, NCH * 128], BF16) for i in range(2)]
        w3c = [sb(f"w3c{i}", [128, NCH * 128], BF16) for i in range(2)]
        w2s = sb("w2s", [128, NOC * 512], BF16)
        woutb = [sb(f"woutb{i}", [128, NCH * 256], BF16) for i in range(2)]
        qldb = [sb(f"qldb{i}", [128, 512], BF16) for i in range(2)]
        sumld = [sb(f"sumld{i}", [128, 256], F32) for i in range(2)]
        outstg = [sb(f"outstg{i}", [128, 256], F32) for i in range(2)]
        xnldb = [sb(f"xnldb{i}", [128, D], BF16) for i in range(2)]
        bob = sb("bob", [128, D], F32)
        gwb = sb("gwb", [128, NCH * 8], BF16)
        gbb = sb("gbb", [128, 8], F32)
        acolb = sb("acolB", [128, 16], F32)
        cwb = sb("cwb", [128, NT_H * 8], F32)
        gat = sb("gat", [128, NT_H * 48], F32)
        sgl = [sb(f"sgl{i}", [128, 512], F32) for i in range(2)]
        cnst = sb("cnstB", [128, 4], F32)
        identb = sb("identbB", [128, 128], BF16)
        junk = sb("junkB", [128, 512], F32)

        pb = [psa(f"pbB{i}", [128, 512], F32) for i in range(8)]
        pbbf = [p[:, :].bitcast(BF16) for p in pb]

        c_ld = sem("c_ldB")
        p_ld = [sem(f"p_ld{i}") for i in range(2)]
        p_tp = sem("p_tp"); p_tev = sem("p_tev")
        g_pe = sem("g_pe"); g_s0 = sem("g_s0"); g_sc = sem("g_sc"); g_ve = sem("g_ve")
        m_w1 = [sem(f"m_w1{i}") for i in range(2)]
        m_w3 = [sem(f"m_w3{i}") for i in range(2)]
        m_ab = sem("m_ab"); m_sg = sem("m_sg"); m_h = sem("m_h")
        m_w2 = sem("m_w2"); m_dn = sem("m_dn"); m_dev = sem("m_dev")
        a_in = sem("a_in")
        g_w = sem("g_w")
        y_q = [sem(f"y_q{i}") for i in range(2)]
        y_w = [sem(f"y_w{i}") for i in range(2)]
        y_y = sem("y_y"); y_pe = sem("y_pe"); y_ev = sem("y_ev")
        s_ld = [sem(f"s_ld{i}") for i in range(2)]
        o_st = [sem(f"o_st{i}") for i in range(2)]

        block = ctx.enter_context(nc.Block())

        def slotval(i, nslots):
            return 16 * (i // nslots + 1)

        def ab_base(h):
            return h * NEXP * NOC * 2
        def h_base(h):
            return h * NEXP * NOC * 2
        def dn_base(h):
            return h * NEXP * 32
        def dev_base(h):
            return h * NEXP * 32
        def yev_base(h):
            return h * 2 * NDG * 4 * 2 // 2

        @block.sync
        def _(sync):
            sync.dma_start(out=cnst[:, :], in_=cns[:, :]).then_inc(c_ld, 16)
            sync.dma_start(out=identb[:, :], in_=ident_bf[:, :]).then_inc(c_ld, 16)
            sync.dma_start(out=bob[:, :], in_=b_out_b[:, :]).then_inc(c_ld, 16)
            sync.dma_start(out=gbb[:, :], in_=gate_bb[:, :]).then_inc(c_ld, 16)
            sync.dma_start(out=acolb[:, :], in_=acol_i[:, :]).then_inc(c_ld, 16)
            for h in range(2):
                for j in range(NT_H):
                    i = h * NT_H + j
                    if i >= 2:
                        sync.wait_ge(p_tp, 16 * (i - 1))
                    sync.dma_start(out=xnldb[i % 2][:, :],
                                   in_=xn1_i[i * 128:(i + 1) * 128, :]).then_inc(p_ld[i % 2], 16)
                for e in range(NEXP):
                    for oc in range(NOC):
                        u = (h * NEXP + e) * NOC + oc
                        if u >= 2:
                            sync.wait_ge(m_ab, (u - 1) * 4)
                        sync.dma_start(out=w1c[u % 2][:, :].rearrange("p (c f) -> p c f", c=NCH),
                                       in_=w1b[e, oc].rearrange("c p f -> p c f")).then_inc(m_w1[u % 2], 16)
                        sync.dma_start(out=w3c[u % 2][:, :].rearrange("p (c f) -> p c f", c=NCH),
                                       in_=w3b[e, oc].rearrange("c p f -> p c f")).then_inc(m_w3[u % 2], 16)
                    for dg in range(4):
                        v = (h * NEXP + e) * 4 + dg
                        if v >= 1:
                            sync.wait_ge(m_dn, v * 8)
                        sync.dma_start(out=w2s[:, :].rearrange("p (c f) -> p c f", c=NOC),
                                       in_=w2t[e, :, dg * 512:(dg + 1) * 512]
                                       .rearrange("(c p) f -> p c f", p=128)).then_inc(m_w2, 16)
                for q in range(2):
                    qb = h * 2 + q
                    for fc in range(NCH):
                        uq = qb * NCH + fc
                        if uq >= 2:
                            sync.wait_ge(y_y, uq - 1)
                        sync.dma_start(out=qldb[uq % 2][:, :],
                                       in_=Q_i[fc * 128:(fc + 1) * 128,
                                               qb * 512:(qb + 1) * 512]).then_inc(y_q[uq % 2], 16)
                    for dg in range(NDG):
                        for sl in range(4):
                            w = (qb * NDG + dg) * 4 + sl
                            if w >= 2:
                                sync.wait_ge(y_ev, w - 1)
                            r0 = qb * 512 + sl * 128
                            sync.dma_start(out=sumld[w % 2][:, :],
                                           in_=sum_i[r0:r0 + 128, dg * 256:(dg + 1) * 256]).then_inc(s_ld[w % 2], 16)
            for q in range(2):
                sync.wait_ge(o_st[q], 16 * 64)

        @block.gpsimd
        def _(gpsimd):
            gpsimd.dma_start(out=gwb[:, :].rearrange("p (c f) -> p c f", c=NCH),
                             in_=gate_wb[:, :, :].rearrange("c p f -> p c f")).then_inc(g_w, 16)
            for h in range(2):
                for q in range(2):
                    qb = h * 2 + q
                    for dg in range(NDG):
                        uw = qb * NDG + dg
                        if uw >= 2:
                            gpsimd.wait_ge(y_pe, (uw - 1) * 4)
                        gpsimd.dma_start(
                            out=woutb[uw % 2][:, :].rearrange("p (c f) -> p c f", c=NCH),
                            in_=W_out_p[:, dg * 256:(dg + 1) * 256].rearrange("(c p) f -> p c f", p=128),
                        ).then_inc(y_w[uw % 2], 16)
                        for sl in range(4):
                            w = (qb * NDG + dg) * 4 + sl
                            gpsimd.wait_ge(y_ev, w + 1)
                            r0 = qb * 512 + sl * 128
                            gpsimd.dma_start(out=out_p[r0:r0 + 128, dg * 256:(dg + 1) * 256],
                                             in_=outstg[w % 2][:, :]).then_inc(o_st[w % 2], 16)
            for q in range(2):
                gpsimd.wait_ge(o_st[q], 16 * 64)

        @block.scalar
        def _(scalar):
            scalar.memzero(junk[:, :])
            for f in (AF.Exp, AF.Exp, AF.Ln, AF.Sqrt, AF.Sqrt):
                scalar.activation(out=junk[:, :], in_=junk[:, :], func=f)
            for h in range(2):
                scalar.wait_ge(g_s0, 8 * (h + 1))
                for j in range(NT_H):
                    g0 = j * 48
                    scalar.activation(out=gat[:, g0 + 8:g0 + 16], in_=gat[:, g0:g0 + 8], func=AF.Abs)
                scalar.activation(out=junk[:, :], in_=junk[:, :], func=AF.Exp)
                scalar.activation(out=junk[:, :], in_=junk[:, :], func=AF.Exp)
                for j in range(NT_H):
                    g0 = j * 48
                    scalar.activation(out=gat[:, g0 + 16:g0 + 24], in_=gat[:, g0 + 8:g0 + 16],
                                      func=AF.Exp, scale=-1.0)
                for j in range(NT_H):
                    g0 = j * 48
                    ins = scalar.activation(out=gat[:, g0 + 16:g0 + 24], in_=gat[:, g0 + 16:g0 + 24],
                                            func=AF.Ln, bias=cnst[:, 1:2])
                ins.then_inc(g_sc, 1)
                scalar.wait_ge(g_ve, h + 1)
                scalar.activation(out=junk[:, :], in_=junk[:, :], func=AF.Sqrt)
                scalar.activation(out=junk[:, :], in_=junk[:, :], func=AF.Sqrt)
                for j in range(NT_H):
                    g0 = j * 48
                    ins = scalar.activation(out=gat[:, g0 + 32:g0 + 40], in_=gat[:, g0 + 24:g0 + 32],
                                            func=AF.Sqrt)
                ins.then_inc(g_sc, 1)
                scalar.activation(out=junk[:, :], in_=junk[:, :], func=AF.Sigmoid)
                scalar.activation(out=junk[:, :], in_=junk[:, :], func=AF.Sigmoid)
                for e in range(NEXP):
                    for oc in range(NOC):
                        for g in range(2):
                            k = ((h * NEXP + e) * NOC + oc) * 2 + g
                            scalar.wait_ge(m_ab, k * 2 - 0 + 1)
                            if k >= 2:
                                scalar.wait_ge(m_h, k - 1)
                            scalar.activation(out=sgl[k % 2][:, :],
                                              in_=pb[(2 * k) % 4][:, :],
                                              func=AF.Sigmoid).then_inc(m_sg, 1)

        @block.vector
        def _(vector):
            vector.wait_ge(c_ld, 16 * 5)
            for h in range(2):
                def emit_evacsB(jj):
                    for b in range(NCH):
                        gb = (h * NT_H + jj) * NCH + b
                        vector.wait_ge(p_tp, gb + 1)
                        vector.tensor_copy(
                            out=xn1T[:, b * HT + jj * 128: b * HT + (jj + 1) * 128],
                            in_=pbbf[gb % 2][:, 0:128]).then_inc(p_tev, 1)

                if h == 1:
                    vector.wait_ge(y_ev, 64)
                for j in range(NT_H):
                    emit_evacsB(j)
                for j in range(NT_H):
                    vector.wait_ge(g_pe, h * NT_H + j + 1)
                    g0 = j * 48
                    vector.tensor_copy(out=gat[:, g0:g0 + 8],
                                       in_=pb[2 + j % 2][:, 0:8]).then_inc(g_s0, 1)
                for sgi in range(NT_H):
                    vector.tensor_copy(out=acc[:, sgi * D:(sgi + 1) * D], in_=bob[:, :])
                vector.wait_ge(g_sc, h * 2 + 1)
                for j in range(NT_H):
                    g0 = j * 48
                    vector.tensor_scalar(out=gat[:, g0 + 24:g0 + 32], in0=gat[:, g0:g0 + 8],
                                         scalar1=0.0, scalar2=None, op0=ALU.max)
                for j in range(NT_H):
                    g0 = j * 48
                    ins = vector.tensor_tensor(out=gat[:, g0 + 24:g0 + 32], in0=gat[:, g0 + 24:g0 + 32],
                                               in1=gat[:, g0 + 16:g0 + 24], op=ALU.add)
                ins.then_inc(g_ve, 1)
                vector.wait_ge(g_sc, h * 2 + 2)
                for j in range(NT_H):
                    g0 = j * 48
                    vector.tensor_tensor(out=gat[:, g0 + 40:g0 + 48], in0=gat[:, g0 + 32:g0 + 40],
                                         in1=gbb[:, :], op=ALU.add)
                for j in range(NT_H):
                    g0 = j * 48
                    vector.reduce_max(gat[:, g0 + 16:g0 + 17], gat[:, g0 + 40:g0 + 48], axis=AX.X)
                for j in range(NT_H):
                    g0 = j * 48
                    vector.tensor_scalar(out=gat[:, g0 + 24:g0 + 32], in0=gat[:, g0 + 40:g0 + 48],
                                         scalar1=gat[:, g0 + 16:g0 + 17], scalar2=None, op0=ALU.is_ge)
                for j in range(NT_H):
                    g0 = j * 48
                    vector.reduce_sum(gat[:, g0 + 17:g0 + 18], gat[:, g0 + 24:g0 + 32], axis=AX.X)
                for j in range(NT_H):
                    g0 = j * 48
                    vector.scalar_tensor_tensor(out=gat[:, g0 + 8:g0 + 16], in0=gat[:, g0 + 24:g0 + 32],
                                                scalar=BIG_NEG, in1=gat[:, g0 + 40:g0 + 48],
                                                op0=ALU.mult, op1=ALU.add)
                for j in range(NT_H):
                    g0 = j * 48
                    vector.reduce_max(gat[:, g0 + 18:g0 + 19], gat[:, g0 + 8:g0 + 16], axis=AX.X)
                for j in range(NT_H):
                    g0 = j * 48
                    vector.tensor_scalar(out=gat[:, g0 + 19:g0 + 20], in0=gat[:, g0 + 17:g0 + 18],
                                         scalar1=2.0, scalar2=None, op0=ALU.is_ge)
                for j in range(NT_H):
                    g0 = j * 48
                    vector.tensor_tensor(out=gat[:, g0 + 20:g0 + 21], in0=gat[:, g0 + 16:g0 + 17],
                                         in1=gat[:, g0 + 18:g0 + 19], op=ALU.subtract)
                for j in range(NT_H):
                    g0 = j * 48
                    vector.tensor_tensor(out=gat[:, g0 + 21:g0 + 22], in0=gat[:, g0 + 19:g0 + 20],
                                         in1=gat[:, g0 + 20:g0 + 21], op=ALU.mult)
                for j in range(NT_H):
                    g0 = j * 48
                    vector.tensor_tensor(out=gat[:, g0 + 22:g0 + 23], in0=gat[:, g0 + 18:g0 + 19],
                                         in1=gat[:, g0 + 21:g0 + 22], op=ALU.add)
                for j in range(NT_H):
                    g0 = j * 48
                    vector.tensor_scalar(out=gat[:, g0 + 24:g0 + 32], in0=gat[:, g0 + 40:g0 + 48],
                                         scalar1=gat[:, g0 + 22:g0 + 23], scalar2=None, op0=ALU.is_ge)
                for j in range(NT_H):
                    g0 = j * 48
                    vector.tensor_tensor(out=cwb[:, j * 8:(j + 1) * 8], in0=gat[:, g0 + 32:g0 + 40],
                                         in1=gat[:, g0 + 24:g0 + 32], op=ALU.mult)
                for e in range(NEXP):
                    for oc in range(NOC):
                        for g in range(2):
                            k = ((h * NEXP + e) * NOC + oc) * 2 + g
                            vector.wait_ge(m_sg, k + 1)
                            vector.tensor_tensor(out=sgl[k % 2][:, :], in0=pb[(2 * k) % 4][:, :],
                                                 in1=sgl[k % 2][:, :], op=ALU.mult)
                            vector.wait_ge(m_ab, k * 2 + 2)
                            vector.tensor_tensor(
                                out=hbuf[:, oc * HT + g * 512:oc * HT + (g + 1) * 512],
                                in0=sgl[k % 2][:, :], in1=pb[(2 * k + 1) % 4][:, :],
                                op=ALU.mult).then_inc(m_h, 1)
                    for dg in range(4):
                        for sgi in range(NT_H):
                            dd = dn_base(h) + (e * 4 + dg) * 8 + sgi
                            vector.wait_ge(m_dn, dd + 1)
                            dst = acc[:, sgi * D + dg * 512: sgi * D + (dg + 1) * 512]
                            if e < 8:
                                vector.scalar_tensor_tensor(
                                    out=dst, in0=pb[4 + dd % 2][:, :],
                                    scalar=cwb[:, sgi * 8 + e:sgi * 8 + e + 1],
                                    in1=dst, op0=ALU.mult, op1=ALU.add).then_inc(m_dev, 1)
                            else:
                                vector.tensor_tensor(out=dst, in0=pb[4 + dd % 2][:, :],
                                                     in1=dst, op=ALU.add).then_inc(m_dev, 1)
                for q in range(2):
                    qb = h * 2 + q
                    for fc in range(NCH):
                        uq = qb * NCH + fc
                        vector.wait_ge(y_q[uq % 2], slotval(uq, 2))
                        if q == 0 and fc == 0:
                            vector.wait_ge(m_h, (h + 1) * NEXP * NOC * 2)
                            vector.wait_ge(m_dn, (h + 1) * NEXP * 32)
                        if fc == 0 and qb >= 1:
                            vector.wait_ge(y_pe, qb * 32)
                        vector.tensor_scalar(out=Ybuf[:, fc * 512:(fc + 1) * 512],
                                             in0=qldb[uq % 2][:, :],
                                             scalar1=acolb[:, fc:fc + 1], scalar2=None,
                                             op0=ALU.mult).then_inc(y_y, 1)
                    for dg in range(NDG):
                        for sl in range(4):
                            w = (qb * NDG + dg) * 4 + sl
                            vector.wait_ge(y_pe, w + 1)
                            vector.wait_ge(s_ld[w % 2], slotval(w, 2))
                            if w >= 2:
                                vector.wait_ge(o_st[w % 2], 16 * (w // 2))
                            sgi = q * 4 + sl
                            vector.tensor_tensor(out=outstg[w % 2][:, :], in0=pb[6 + w % 2][:, 0:256],
                                                 in1=acc[:, sgi * D + dg * 256: sgi * D + (dg + 1) * 256],
                                                 op=ALU.add)
                            vector.tensor_tensor(out=outstg[w % 2][:, :], in0=outstg[w % 2][:, :],
                                                 in1=sumld[w % 2][:, :], op=ALU.add).then_inc(y_ev, 1)

        @block.tensor
        def _(tensor):
            tensor.wait_ge(c_ld, 16 * 5)
            tensor.wait_ge(g_w, 16)
            for h in range(2):
                if h >= 1:
                    tensor.wait_ge(m_h, h * NEXP * NOC * 2)
                for j in range(NT_H):
                    i = h * NT_H + j
                    tensor.wait_ge(p_ld[i % 2], slotval(i, 2))
                    for b in range(NCH):
                        gb = i * NCH + b
                        if gb >= 2:
                            tensor.wait_ge(p_tev, gb - 1)
                        tensor.transpose(out=pbbf[gb % 2][:, 0:128],
                                         in_=xnldb[i % 2][:, b * 128:(b + 1) * 128],
                                         identity=identb[:, :]).then_inc(p_tp, 1)
                for j in range(NT_H):
                    tensor.wait_ge(p_tev, (h * NT_H + j + 1) * NCH)
                    if h * NT_H + j >= 2:
                        tensor.wait_ge(g_s0, h * NT_H + j - 1)
                    for dc in range(NCH):
                        ins = tensor.matmul(out=pb[2 + j % 2][:, 0:8],
                                            lhsT=xn1T[:, dc * HT + j * 128:(dc) * HT + (j + 1) * 128],
                                            rhs=gwb[:, dc * 8:(dc + 1) * 8],
                                            start=(dc == 0), stop=(dc == NCH - 1))
                    ins.then_inc(g_pe, 1)
                for e in range(NEXP):
                    for oc in range(NOC):
                        u = (h * NEXP + e) * NOC + oc
                        if e == 0 and oc == 0:
                            tensor.wait_ge(p_tev, 128 * (h + 1))
                            tensor.wait_ge(g_s0, 8 * (h + 1))
                        tensor.wait_ge(m_w1[u % 2], slotval(u, 2))
                        tensor.wait_ge(m_w3[u % 2], slotval(u, 2))
                        for g in range(2):
                            k = u * 2 + g
                            if k >= 2:
                                tensor.wait_ge(m_h, k - 1)
                            for dc in range(NCH):
                                ins = tensor.matmul(
                                    out=pb[(2 * k) % 4][:, :],
                                    lhsT=w1c[u % 2][:, dc * 128:(dc + 1) * 128],
                                    rhs=xn1T[:, dc * HT + g * 512: dc * HT + (g + 1) * 512],
                                    start=(dc == 0), stop=(dc == NCH - 1))
                            ins.then_inc(m_ab, 1)
                            for dc in range(NCH):
                                ins = tensor.matmul(
                                    out=pb[(2 * k + 1) % 4][:, :],
                                    lhsT=w3c[u % 2][:, dc * 128:(dc + 1) * 128],
                                    rhs=xn1T[:, dc * HT + g * 512: dc * HT + (g + 1) * 512],
                                    start=(dc == 0), stop=(dc == NCH - 1))
                            ins.then_inc(m_ab, 1)
                    for dg in range(4):
                        v = (h * NEXP + e) * 4 + dg
                        tensor.wait_ge(m_w2, 16 * (v + 1))
                        for sgi in range(NT_H):
                            dd = dn_base(h) + (e * 4 + dg) * 8 + sgi
                            if dg == 0 and sgi == 0:
                                tensor.wait_ge(m_h, (h * NEXP + e + 1) * NOC * 2)
                            if dd >= 2:
                                tensor.wait_ge(m_dev, dd - 1)
                            for oc2 in range(NOC):
                                ins = tensor.matmul(
                                    out=pb[4 + dd % 2][:, :],
                                    lhsT=hbuf[:, oc2 * HT + sgi * 128: oc2 * HT + sgi * 128 + 128],
                                    rhs=w2s[:, oc2 * 512:(oc2 + 1) * 512],
                                    start=(oc2 == 0), stop=(oc2 == NOC - 1))
                            ins.then_inc(m_dn, 1)
                for q in range(2):
                    qb = h * 2 + q
                    for dg in range(NDG):
                        uw = qb * NDG + dg
                        tensor.wait_ge(y_w[uw % 2], slotval(uw, 2))
                        for sl in range(4):
                            w = (qb * NDG + dg) * 4 + sl
                            if sl == 0:
                                tensor.wait_ge(y_y, (qb + 1) * NCH)
                            if w >= 2:
                                tensor.wait_ge(y_ev, w - 1)
                            for fc in range(NCH):
                                ins = tensor.matmul(
                                    out=pb[6 + w % 2][:, 0:256],
                                    lhsT=Ybuf[:, fc * 512 + sl * 128: fc * 512 + (sl + 1) * 128],
                                    rhs=woutb[uw % 2][:, fc * 256:(fc + 1) * 256],
                                    start=(fc == 0), stop=(fc == NCH - 1))
                            ins.then_inc(y_pe, 1)

    return nc


def _l1b_core_inputs(inputs, c, xn1, Q, acol, sm):
    sig = _sigma_perm()
    gate_w = np.asarray(inputs["L1_ffn_gate_w"], dtype=np.float32)
    gate_wb = np.ascontiguousarray(gate_w.T.reshape(16, 128, 8))
    gate_bb = np.ascontiguousarray(np.broadcast_to(np.asarray(inputs["L1_ffn_gate_b"], dtype=np.float32), (128, 8)))
    b_out = np.asarray(inputs["L1_attn_b_out"], dtype=np.float32)
    b_out_b = np.ascontiguousarray(np.broadcast_to(b_out, (128, D)))
    w1 = np.asarray(inputs["L1_ffn_w1"])
    w3 = np.asarray(inputs["L1_ffn_w3"])
    w2 = np.asarray(inputs["L1_ffn_w2"])
    sw1 = np.asarray(inputs["L1_ffn_sw1"])
    sw3 = np.asarray(inputs["L1_ffn_sw3"])
    sw2 = np.asarray(inputs["L1_ffn_sw2"])
    w1a = np.concatenate([w1, sw1[None]], axis=0)
    w3a = np.concatenate([w3, sw3[None]], axis=0)
    w2a = np.concatenate([w2, sw2[None]], axis=0)
    w1b = np.ascontiguousarray(w1a.reshape(NEXP, NOC, 128, NCH, 128).transpose(0, 1, 3, 4, 2))
    w3b = np.ascontiguousarray(w3a.reshape(NEXP, NOC, 128, NCH, 128).transpose(0, 1, 3, 4, 2))
    w2tt = np.ascontiguousarray(w2a.transpose(0, 2, 1))
    W_out = np.asarray(inputs["L1_attn_W_out"], dtype=np.float32)
    W_out_p = np.ascontiguousarray(W_out[sig, :])
    cns = np.zeros((128, 4), dtype=np.float32)
    cns[:, 0] = EPS
    cns[:, 1] = 1.0
    return {"xn1_i": np.ascontiguousarray(xn1), "Q_i": np.ascontiguousarray(Q),
            "acol_i": np.ascontiguousarray(acol), "sum_i": np.ascontiguousarray(sm),
            "gate_wb": gate_wb, "gate_bb": gate_bb, "b_out_b": b_out_b,
            "w1b": w1b, "w3b": w3b, "w2t": w2tt, "W_out_p": W_out_p,
            "ident_bfB": np.eye(128, dtype=BF16NP), "cnsB": cns}
